# version 13
# baseline (speedup 1.0000x reference)
import contextlib
import math
import numpy as np
import ml_dtypes
import concourse.bass as bass
import concourse.mybir as mybir
from concourse.bass_utils import run_bass_kernel_spmd

F32 = mybir.dt.float32
BF16 = mybir.dt.bfloat16
AF = mybir.ActivationFunctionType
ALU = mybir.AluOpType
AX = mybir.AxisListType

D = 2048
H = 16
HD = 128
S = 4096
DH = 8192
NT = 32
NO = 16
INV_SQRT_HD = 1.0 / math.sqrt(HD)

ENGS = ("sync", "gpsimd", "vector", "scalar", "tensor")
DBG = ()


class Plan:
    def __init__(self, sems):
        self.sems = sems
        self.counts = {k: 0 for k in sems}
        self.ops = {e: [] for e in ENGS}
        self.waited = {e: {k: 0 for k in sems} for e in ENGS}

    def new_block(self):
        self.ops = {e: [] for e in ENGS}

    def op(self, eng, fn, inc=None, waits=()):
        for w in waits:
            if w is None:
                continue
            s, v = w
            if v > self.waited[eng][s]:
                self.waited[eng][s] = v
                self.ops[eng].append(
                    lambda e, _s=self.sems[s], _v=v: e.wait_ge(_s, _v))
        if inc is None:
            self.ops[eng].append(fn)
            return None
        s, n = inc
        self.counts[s] += n
        self.ops[eng].append(
            lambda e, _f=fn, _s=self.sems[s], _n=n: _f(e).then_inc(_s, _n))
        return (s, self.counts[s])

    def dma(self, eng, out, in_, waits=(), sem=None):
        if sem is None:
            sem = {"sync": "sq", "gpsimd": "gq", "scalar": "aq"}[eng]
        return self.op(eng, lambda e, _o=out, _i=in_: e.dma_start(out=_o, in_=_i),
                       inc=(sem, 16), waits=waits)

    def emit(self, block):
        for eng in ENGS:
            lst = list(self.ops[eng])
            if not lst:
                continue
            def body(e, _lst=lst):
                for f in _lst:
                    f(e)
            getattr(block, eng)(body)


def build_nc():
    nc = bass.Bass("TRN2", target_bir_lowering=False)

    x_perm = nc.dram_tensor("x_perm", [S, D], F32, kind="ExternalInput").ap()
    wqT = nc.dram_tensor("wqT", [D, D], F32, kind="ExternalInput").ap()
    wkT = nc.dram_tensor("wkT", [D, D], F32, kind="ExternalInput").ap()
    wvT = nc.dram_tensor("wvT", [D, D], F32, kind="ExternalInput").ap()
    woT = nc.dram_tensor("woT", [D, D], F32, kind="ExternalInput").ap()
    wgT = nc.dram_tensor("wgT", [D, DH], F32, kind="ExternalInput").ap()
    wuT = nc.dram_tensor("wuT", [D, DH], F32, kind="ExternalInput").ap()
    wdT = nc.dram_tensor("wdT", [DH, D], F32, kind="ExternalInput").ap()
    bq_t = nc.dram_tensor("bq_t", [128, 16], F32, kind="ExternalInput").ap()
    bk_t = nc.dram_tensor("bk_t", [128, 16], F32, kind="ExternalInput").ap()
    bv_row = nc.dram_tensor("bv_row", [1, D], F32, kind="ExternalInput").ap()
    bo_row = nc.dram_tensor("bo_row", [1, D], F32, kind="ExternalInput").ap()
    tri_m = nc.dram_tensor("tri_m", [128, 128], F32, kind="ExternalInput").ap()
    mask_b = nc.dram_tensor("mask_b", [128, 128], F32, kind="ExternalInput").ap()
    ident = nc.dram_tensor("ident", [128, 128], F32, kind="ExternalInput").ap()
    ones_col = nc.dram_tensor("ones_col", [128, 1], F32, kind="ExternalInput").ap()
    ones_row = nc.dram_tensor("ones_row", [1, 128], F32, kind="ExternalInput").ap()
    neg_ones_row = nc.dram_tensor("neg_ones_row", [1, 128], F32, kind="ExternalInput").ap()

    y_out = nc.dram_tensor("y", [NO * 128, D], F32, kind="ExternalOutput").ap()

    xnT_s = [nc.dram_tensor(f"xnT_s{c}", [128, S], BF16, kind="Internal").ap() for c in range(16)]
    qT_s = [nc.dram_tensor(f"qT_s{h}", [128, 2048], BF16, kind="Internal").ap() for h in range(H)]
    kT_s = [nc.dram_tensor(f"kT_s{h}", [128, S], BF16, kind="Internal").ap() for h in range(H)]
    v_s = [nc.dram_tensor(f"v_s{h}", [S, HD], BF16, kind="Internal").ap() for h in range(H)]
    attnT_s = [nc.dram_tensor(f"attnT_s{h}", [128, 2048], BF16, kind="Internal").ap() for h in range(H)]
    gatedT_s = [nc.dram_tensor(f"gatedT_s{hc}", [128, 2048], BF16, kind="Internal").ap() for hc in range(64)]
    wd_bf_s = nc.dram_tensor("wd_bf_s", [DH, D], BF16, kind="Internal").ap()
    wo_bf_s = nc.dram_tensor("wo_bf_s", [D, D], BF16, kind="Internal").ap()

    dbg = {}
    if "xnT" in DBG:
        dbg["xnT"] = [nc.dram_tensor(f"dbg_xnT{c}", [128, S], BF16, kind="ExternalOutput").ap() for c in range(16)]
    if "qkv" in DBG:
        dbg["q"] = [nc.dram_tensor(f"dbg_q{h}", [128, 2048], BF16, kind="ExternalOutput").ap() for h in range(H)]
        dbg["k"] = [nc.dram_tensor(f"dbg_k{h}", [128, S], BF16, kind="ExternalOutput").ap() for h in range(H)]
        dbg["v"] = [nc.dram_tensor(f"dbg_v{h}", [S, HD], BF16, kind="ExternalOutput").ap() for h in range(H)]
        dbg["nq"] = nc.dram_tensor("dbg_nq", [1, 16], F32, kind="ExternalOutput").ap()
        dbg["nk"] = nc.dram_tensor("dbg_nk", [1, 16], F32, kind="ExternalOutput").ap()
    if "attnT" in DBG:
        dbg["attnT"] = [nc.dram_tensor(f"dbg_attnT{h}", [128, 2048], BF16, kind="ExternalOutput").ap() for h in range(H)]
    if "gatedT" in DBG:
        dbg["gatedT"] = [nc.dram_tensor(f"dbg_gatedT{hc}", [128, 2048], BF16, kind="ExternalOutput").ap() for hc in range(64)]
    if "p5in" in DBG:
        dbg["at_sb"] = nc.dram_tensor("dbg_at_sb", [128, 16 * 2048], BF16, kind="ExternalOutput").ap()
        dbg["gt_sb"] = nc.dram_tensor("dbg_gt_sb", [128, 32 * 256], BF16, kind="ExternalOutput").ap()

    top = contextlib.ExitStack()
    sbuf = lambda name, shape, dt=F32: top.enter_context(nc.sbuf_tensor(name, shape, dt))

    with top:
        sems = {}
        for name in ("sq", "gq", "aq", "pe", "act", "dve"):
            sems[name] = top.enter_context(nc.semaphore(name))
        P = Plan(sems)

        tri_sb = sbuf("tri_sb", [128, 128])
        maskb_sb = sbuf("maskb_sb", [128, 128])
        tri_bf = sbuf("tri_bf", [128, 128], BF16)
        maskb_bf = sbuf("maskb_bf", [128, 128], BF16)
        id_bf = sbuf("id_bf", [128, 128], BF16)
        onesc_f = sbuf("onesc_f", [128, 1])
        onesr_bf = sbuf("onesr_bf", [1, 128], BF16)
        negonesr_f = sbuf("negonesr_f", [1, 128])
        bqs_sb = sbuf("bqs_sb", [128, 16])
        bk_sb = sbuf("bk_sb", [128, 16])
        bv_bf = sbuf("bv_bf", [1, D], BF16)
        bo_bf = sbuf("bo_bf", [1, D], BF16)
        c_invD = sbuf("c_invD", [128, 1])
        c_eps = sbuf("c_eps", [128, 1])
        c_isqhd = sbuf("c_isqhd", [128, 1])
        c_mscale = sbuf("c_mscale", [1, 1])
        nq_sb = sbuf("nq_sb", [1, H])
        nk_sb = sbuf("nk_sb", [1, H])

        P.new_block()
        P.dma("sync", tri_sb[:], tri_m)
        P.dma("sync", maskb_sb[:], mask_b)
        P.dma("sync", negonesr_f[:], neg_ones_row)
        P.dma("sync", onesc_f[:], ones_col)
        P.dma("sync", bk_sb[:], bk_t)
        bq_ev = P.dma("sync", bqs_sb[:], bq_t)
        P.dma("gpsimd", id_bf[:], ident)
        P.dma("gpsimd", tri_bf[:], tri_m)
        P.dma("gpsimd", maskb_bf[:], mask_b)
        P.dma("gpsimd", onesr_bf[:], ones_row)
        P.dma("gpsimd", bv_bf[:], bv_row)
        consts_gq = P.dma("gpsimd", bo_bf[:], bo_row)
        P.op("vector", lambda v: v.memset(c_invD[:], 1.0 / D))
        P.op("vector", lambda v: v.memset(c_eps[:], 1e-5))
        P.op("vector", lambda v: v.memset(c_mscale[:], 4.0 / (float(D) * float(S))))
        ms_ev = P.op("vector", lambda v: v.memset(c_isqhd[:], INV_SQRT_HD), inc=("dve", 1))
        bqs_ev = P.op("vector",
                      lambda v: v.tensor_scalar(bqs_sb[:], bqs_sb[:], c_isqhd[:, 0:1], None, ALU.mult),
                      inc=("dve", 1), waits=[bq_ev, ms_ev])
        consts_sq = bq_ev
        with nc.Block() as blk:
            P.emit(blk)

        P.new_block()
        grp_dma = {}
        with contextlib.ExitStack() as ph:
            psb = lambda n, s, dt=F32: ph.enter_context(nc.sbuf_tensor(n, s, dt))
            pps = lambda n, s: ph.enter_context(nc.psum_tensor(n, s, F32))
            x_sb = [psb(f"p1x{i}", [128, D]) for i in range(3)]
            xsq_sb = psb("p1sq", [128, D], BF16)
            stat = [psb(f"p1st{i}", [128, 8]) for i in range(2)]
            xn_bf = [psb(f"p1xn{i}", [128, D], BF16) for i in range(2)]
            stage = [psb(f"p1stage{i}", [128, 16 * 512], BF16) for i in range(2)]
            ps_tr = [pps(f"p1tr{i}", [128, 512]) for i in range(2)]

            xn_done, evac_done, sq_done = {}, {}, {}
            for t in range(NT):
                g, tt = divmod(t, 4)
                xb, stt, xnb, stg = x_sb[t % 3], stat[t % 2], xn_bf[t % 2], stage[g % 2]
                x_ev = P.dma("sync", xb[:], x_perm[128 * t:128 * (t + 1), :],
                             waits=[xn_done.get(t - 3)])
                s1 = P.op("vector", lambda v, xb=xb, stt=stt: v.reduce_sum(
                    stt[:, 0:1], xb[:], axis=AX.X),
                    inc=("dve", 1), waits=[x_ev, xn_done.get(t - 2)])
                sq_ev = P.op("scalar", lambda s, xb=xb, stt=stt: s.activation(
                    xsq_sb[:], xb[:], AF.Square, accum_out=stt[:, 1:2]),
                    inc=("act", 1), waits=[x_ev, xn_done.get(t - 2), sq_done.get(t - 1)])
                sq_done[t] = sq_ev
                c1 = P.op("vector", lambda v, stt=stt: v.tensor_scalar(
                    stt[:, 2:3], stt[:, 0:1], c_invD[:, 0:1], None, ALU.mult),
                    inc=("dve", 1), waits=[s1])
                c3 = P.op("vector", lambda v, stt=stt: v.tensor_tensor(
                    stt[:, 4:5], stt[:, 2:3], stt[:, 2:3], ALU.mult),
                    inc=("dve", 1), waits=[c1])
                c4 = P.op("vector", lambda v, stt=stt: v.scalar_tensor_tensor(
                    stt[:, 5:6], stt[:, 1:2], c_invD[:, 0:1], stt[:, 4:5], ALU.mult, ALU.subtract),
                    inc=("dve", 1), waits=[c3, sq_ev])
                c5 = P.op("vector", lambda v, stt=stt: v.tensor_tensor(
                    stt[:, 5:6], stt[:, 5:6], c_eps[:], ALU.add),
                    inc=("dve", 1), waits=[c4])
                c6 = P.op("scalar", lambda s, stt=stt: s.sqrt(stt[:, 6:7], stt[:, 5:6]),
                          inc=("act", 1), waits=[c5])
                c7 = P.op("vector", lambda v, stt=stt: v.reciprocal(stt[:, 6:7], stt[:, 6:7]),
                          inc=("dve", 1), waits=[c6])
                c8 = P.op("vector", lambda v, stt=stt: v.scalar_tensor_tensor(
                    stt[:, 7:8], stt[:, 2:3], -1.0, stt[:, 6:7], ALU.mult, ALU.mult),
                    inc=("dve", 1), waits=[c7])
                xn_ev = P.op("scalar", lambda s, xb=xb, xnb=xnb, stt=stt: s.activation(
                    xnb[:], xb[:], AF.Identity, bias=stt[:, 7:8], scale=stt[:, 6:7]),
                    inc=("act", 1), waits=[c8, evac_done.get((t - 2, 3))])
                xn_done[t] = xn_ev
                for pg in range(4):
                    pb = ps_tr[pg % 2]
                    prev_ev = evac_done.get((t, pg - 2)) or evac_done.get((t - 1, pg + 2))
                    tr_ev = None
                    for i in range(4):
                        c = 4 * pg + i
                        tr_ev = P.op("tensor", lambda te, xnb=xnb, pb=pb, c=c, i=i: te.matmul(
                            pb[:, 128 * i:128 * (i + 1)],
                            xnb[:, 128 * c:128 * (c + 1)], id_bf[:],
                            start=True, stop=True),
                            inc=("pe", 1) if i == 3 else None,
                            waits=([xn_ev, prev_ev, consts_gq] if i == 0 else []))
                    stg3 = stg[:].rearrange("p (c w) -> p c w", w=512)
                    pb3 = pb[:].rearrange("p (c w) -> p c w", w=128)
                    ev = P.op("scalar", lambda s, stg3=stg3, pb3=pb3, pg=pg, tt=tt: s.activation(
                        stg3[:, 4 * pg:4 * pg + 4, 128 * tt:128 * (tt + 1)], pb3[:, :, :], AF.Copy),
                        inc=("act", 1), waits=[tr_ev, grp_dma.get(g - 2)])
                    evac_done[(t, pg)] = ev
                if tt == 3:
                    last_ev = evac_done[(t, 3)]
                    for c in range(16):
                        grp_dma[g] = P.dma("scalar", xnT_s[c][:, 512 * g:512 * (g + 1)],
                                           stg[:, 512 * c:512 * (c + 1)], waits=[last_ev])
            if "xnT" in dbg:
                aq_all = ("aq", P.counts["aq"])
                for c in range(16):
                    P.dma("scalar", dbg["xnT"][c], xnT_s[c], waits=[aq_all])
            with nc.Block() as blk:
                P.emit(blk)

        P.new_block()
        with contextlib.ExitStack() as ph:
            psb = lambda n, s, dt=F32: ph.enter_context(nc.sbuf_tensor(n, s, dt))
            pps = lambda n, s: ph.enter_context(nc.psum_tensor(n, s, F32))
            w_sl = [[psb(f"p2w{j}{i}", [128, 16 * 512], BF16) for i in range(2)] for j in range(3)]
            xg_sb = [psb(f"p2xg{i}", [128, 16 * 512], BF16) for i in range(2)]
            kq_buf = [psb(f"p2kq{i}", [128, 512], BF16) for i in range(4)]
            v_buf = [psb(f"p2v{i}", [128, 512], BF16) for i in range(2)]
            sqd_sb = psb("p2sqd", [128, 512], BF16)
            nacc = psb("p2nacc", [128, 2])
            ps_kq = [pps(f"p2pkq{i}", [128, 512]) for i in range(2)]
            ps_v = [pps(f"p2pv{i}", [128, 512]) for i in range(2)]
            ps_n_full = pps("p2pn", [128, 512])

            w_ld = {}
            slab_used = {}
            xg_ev = {}
            xg_used = {}
            kq_evac = {}
            kq_dmae = {}
            v_evac = {}
            v_dmae = {}
            last_sq = consts_sq
            last_nmm = {0: None, 1: None}
            n_first = [True]
            n_evt = None
            unit_ctr = [0]

            def load_slabs(hg):
                ev = None
                for j, wt in enumerate((wqT, wkT, wvT)):
                    wsb = w_sl[j][hg % 2]
                    for c in range(16):
                        ev = P.dma("gpsimd", wsb[:, 512 * c:512 * (c + 1)],
                                   wt[128 * c:128 * (c + 1), 512 * hg:512 * (hg + 1)],
                                   waits=[slab_used.get(hg - 2), consts_gq])
                w_ld[hg] = ev

            load_slabs(0)
            for hg in range(4):
                if hg < 3:
                    load_slabs(hg + 1)
                wq_sb, wk_sb, wv_sb = (w_sl[j][hg % 2] for j in range(3))
                for g in range(8):
                    xg = xg_sb[g % 2]
                    xev = None
                    for c in range(16):
                        xev = P.dma("sync", xg[:, 512 * c:512 * (c + 1)],
                                    xnT_s[c][:, 512 * g:512 * (g + 1)],
                                    waits=[grp_dma[g] if hg == 0 else None,
                                           xg_used.get((hg, g - 2)) or xg_used.get((hg - 1, g + 6))])
                    xg_ev[(hg, g)] = xev
                    for hl in range(4):
                        h = 4 * hg + hl
                        for isq in (False, True):
                            if isq and g >= 4:
                                continue
                            wsb = wq_sb if isq else wk_sb
                            scr = qT_s if isq else kT_s
                            u = unit_ctr[0]; unit_ctr[0] += 1
                            pp = ps_kq[u % 2]
                            kqb = kq_buf[u % 4]
                            mm = None
                            for c in range(16):
                                mm = P.op("tensor", lambda te, pp=pp, wsb=wsb, xg=xg, c=c, hl=hl: te.matmul(
                                    pp[:], wsb[:, 512 * c + 128 * hl:512 * c + 128 * (hl + 1)],
                                    xg[:, 512 * c:512 * (c + 1)],
                                    start=(c == 0), stop=(c == 15)),
                                    inc=("pe", 1) if c == 15 else None,
                                    waits=([w_ld[hg], xev, kq_evac.get(id(pp))] if c == 0 else []))
                            slab_used[hg] = mm
                            if isq:
                                ev = P.op("scalar", lambda s, kqb=kqb, pp=pp, h=h: s.activation(
                                    kqb[:], pp[:], AF.Identity, bias=bqs_sb[:, h:h + 1], scale=c_isqhd[:, 0:1]),
                                    inc=("act", 1), waits=[mm, kq_dmae.get(id(kqb)), bqs_ev])
                            else:
                                ev = P.op("scalar", lambda s, kqb=kqb, pp=pp, h=h: s.activation(
                                    kqb[:], pp[:], AF.Identity, bias=bk_sb[:, h:h + 1]),
                                    inc=("act", 1), waits=[mm, kq_dmae.get(id(kqb))])
                            kq_evac[id(pp)] = ev
                            col = 0 if isq else 1
                            sq2 = P.op("scalar", lambda s, kqb=kqb, col=col: s.activation(
                                sqd_sb[:], kqb[:], AF.Square, accum_out=nacc[:, col:col + 1]),
                                inc=("act", 1), waits=[ev, last_nmm[col]])
                            last_sq = sq2
                            stv = n_first[0]; n_first[0] = False
                            ncol = h if isq else 64 + h
                            nmm = P.op("tensor", lambda te, ncol=ncol, stv=stv, col=col: te.matmul(
                                ps_n_full[0:1, ncol:ncol + 1], onesc_f[:], nacc[:, col:col + 1],
                                start=stv, stop=False),
                                inc=("pe", 1), waits=[sq2])
                            last_nmm[col] = nmm
                            n_evt = nmm
                            dm = P.dma("scalar", scr[h][:, 512 * g:512 * (g + 1)], kqb[:], waits=[ev])
                            kq_dmae[id(kqb)] = dm
                    for tt in range(4):
                        t = 4 * g + tt
                        pp = ps_v[tt % 2]
                        vb = v_buf[tt % 2]
                        for c in range(16):
                            P.op("tensor", lambda te, pp=pp, xg=xg, wv_sb=wv_sb, c=c, tt=tt: te.matmul(
                                pp[:], xg[:, 512 * c + 128 * tt:512 * c + 128 * (tt + 1)],
                                wv_sb[:, 512 * c:512 * (c + 1)],
                                start=(c == 0), stop=False),
                                inc=None,
                                waits=([w_ld[hg], xev, v_evac.get(id(pp))] if c == 0 else []))
                        mm = P.op("tensor", lambda te, pp=pp, hg=hg: te.matmul(
                            pp[:], onesr_bf[0:1, :], bv_bf[0:1, 512 * hg:512 * (hg + 1)],
                            start=False, stop=True),
                            inc=("pe", 1))
                        slab_used[hg] = mm
                        xg_used[(hg, g)] = mm
                        ev = P.op("vector", lambda v, vb=vb, pp=pp: v.tensor_copy(vb[:], pp[:]),
                                  inc=("dve", 1), waits=[mm, v_dmae.get(id(vb))])
                        v_evac[id(pp)] = ev
                        for hl in range(4):
                            h = 4 * hg + hl
                            v_dmae[id(vb)] = P.dma("scalar", v_s[h][128 * t:128 * (t + 1), :],
                                                   vb[:, 128 * hl:128 * (hl + 1)], waits=[ev])
            nf_q = P.op("scalar", lambda s: s.copy(nq_sb[:], ps_n_full[0:1, 0:H]),
                        inc=("act", 1), waits=[n_evt])
            nf_k = P.op("scalar", lambda s: s.copy(nk_sb[:], ps_n_full[0:1, 64:64 + H]),
                        inc=("act", 1), waits=[n_evt])
            if "q" in dbg:
                aq_all = ("aq", P.counts["aq"])
                for h in range(H):
                    P.dma("scalar", dbg["q"][h], qT_s[h], waits=[nf_k, aq_all])
                    P.dma("scalar", dbg["k"][h], kT_s[h], waits=[nf_k, aq_all])
                    P.dma("scalar", dbg["v"][h], v_s[h], waits=[nf_k, aq_all])
                P.dma("scalar", dbg["nq"], nq_sb[0:1, 0:H], waits=[nf_k, aq_all])
                P.dma("scalar", dbg["nk"], nk_sb[0:1, 0:H], waits=[nf_k, aq_all])
            with nc.Block() as blk:
                P.emit(blk)
        p2_aq = ("aq", P.counts["aq"])

        P.new_block()
        with contextlib.ExitStack() as ph:
            psb = lambda n, s, dt=F32: ph.enter_context(nc.sbuf_tensor(n, s, dt))
            pps = lambda n, s: ph.enter_context(nc.psum_tensor(n, s, F32))
            kt_b = [psb(f"p3kt{i}", [128, S], BF16) for i in range(2)]
            qt_b = [psb(f"p3qt{i}", [128, 2048], BF16) for i in range(2)]
            v_b = [psb(f"p3v{i}", [128, 32 * 130], BF16) for i in range(2)]
            pt_b = [psb(f"p3pt{i}", [128, 512], BF16) for i in range(2)]
            rc_b = psb("p3rc", [128, 16])
            m1_b = psb("p3m1", [1, 1])
            negM = psb("p3negM", [128, 2])
            at_b = [psb(f"p3at{i}", [128, 128], BF16) for i in range(2)]
            ast_b = [psb(f"p3ast{i}", [128, 2048], BF16) for i in range(2)]
            conv_b = [psb(f"p3cv{i}", [128, 2048], BF16) for i in range(2)]
            acc = [pps(f"p3acc{i}", [128, 512]) for i in range(6)]
            sc = [pps(f"p3sc{i}", [128, 512]) for i in range(2)]

            def slot_ap(s):
                return acc[s // 3 if s < 15 else 5], 170 * (s % 3) if s < 15 else 0

            vm_ev = None
            for i in range(2):
                for t in range(32):
                    vm_ev = P.op("vector", lambda v, i=i, t=t: v.memset(
                        v_b[i][:, 130 * t + 128:130 * t + 129], 1.0),
                        inc=("dve", 1) if (i == 1 and t == 31) else None)

            conv_jobs = [(wd_bf_s[128 * c:128 * (c + 1), :], wdT[128 * c:128 * (c + 1), :]) for c in range(64)]
            conv_jobs += [(wo_bf_s[128 * c:128 * (c + 1), :], woT[128 * c:128 * (c + 1), :]) for c in range(16)]
            conv_st = {}

            m4_prev = [None]
            head_loads = {}
            head_attn_done = {}
            head_evac_done = {}
            head_trev_done = {}
            ast_dma = {}

            def load_head(h):
                i = h % 2
                w = [head_attn_done.get(h - 2)]
                if h < 2:
                    w.append(p2_aq)
                kev = P.dma("sync", kt_b[i][:], kT_s[h], waits=w)
                qev = P.dma("sync", qt_b[i][:], qT_s[h], waits=w)
                vev = None
                for t in range(32):
                    vev = P.dma("sync", v_b[i][:, 130 * t:130 * t + 128],
                                v_s[h][128 * t:128 * (t + 1), :], waits=w)
                head_loads[h] = vev

            load_head(0)
            for h in range(H):
                i = h % 2
                if h + 1 < H:
                    load_head(h + 1)
                for j in range(5 * h, min(5 * (h + 1), len(conv_jobs))):
                    dst, src = conv_jobs[j]
                    cb = conv_b[j % 2]
                    lev = P.dma("gpsimd", cb[:], src, waits=[conv_st.get(j - 2), consts_gq])
                    conv_st[j] = P.dma("sync", dst, cb[:], waits=[lev])
                mw = [nf_q, nf_k, m4_prev[0]]
                m1e = P.op("vector", lambda v, h=h: v.tensor_tensor(
                    m1_b[:], nq_sb[0:1, h:h + 1], nk_sb[0:1, h:h + 1], ALU.mult),
                    inc=("dve", 1), waits=mw)
                m2e = P.op("scalar", lambda s: s.activation(
                    m1_b[:], m1_b[:], AF.Sqrt, scale=c_mscale[0:1, 0:1]),
                    inc=("act", 1), waits=[m1e])
                m3e = P.op("tensor", lambda te, h=h: te.matmul(
                    acc[5][0:128, 300:301], negonesr_f[0:1, :], m1_b[0:1, 0:1],
                    start=True, stop=True),
                    inc=("pe", 1), waits=[m2e, head_evac_done.get(h - 1)])
                m4e = P.op("scalar", lambda s, h=h: s.copy(negM[:, h % 2:h % 2 + 1], acc[5][:, 300:301]),
                           inc=("act", 1), waits=[m3e])
                m4_prev[0] = m4e

                sc_free = {0: head_trev_done.get(h - 1), 1: head_trev_done.get(h - 1)}
                pt_free = {0: None, 1: None}
                chunk_idx = 0
                lastPV = {}
                exp_evs = []
                pending = None

                def emit_pvs(pend):
                    ci, lo, hi, kpos, ee = pend
                    pvw = [ee]
                    pv = None
                    for s_ in range(lo // 128, hi // 128):
                        bank, off = slot_ap(s_)
                        if kpos == 0 and s_ == 15:
                            pvw = pvw + [m4_prev[0]]
                        pv = P.op("tensor", lambda te, bank=bank, off=off, ci=ci, s_=s_, lo=lo, kpos=kpos, i=i: te.matmul(
                            bank[:, off:off + 129],
                            pt_b[ci][:, 128 * s_ - lo:128 * s_ - lo + 128],
                            v_b[i][:, 130 * kpos:130 * kpos + 129],
                            start=(kpos == 0 and s_ % 3 == 0), stop=(kpos == 16 + s_)),
                            inc=("pe", 1), waits=pvw)
                        pvw = []
                        lastPV[s_] = pv
                    pt_free[ci] = pv

                for kpos in range(32):
                    p = kpos % 16
                    kw = [head_loads[h], vm_ev]
                    if h == 0:
                        kw.append(head_loads.get(1))
                    for blk_i in range(p // 4, 4):
                        lo = max(128 * p, 512 * blk_i)
                        hi = 512 * (blk_i + 1)
                        W = hi - lo
                        ci = chunk_idx % 2
                        chunk_idx += 1
                        smw = kw + [sc_free[ci], head_evac_done.get(h - 1) if kpos == 0 else None]
                        diag = (lo == 128 * p)
                        sm = P.op("tensor", lambda te, ci=ci, kpos=kpos, lo=lo, W=W, i=i, diag=diag: te.matmul(
                            sc[ci][:, 0:W], kt_b[i][:, 128 * kpos:128 * (kpos + 1)],
                            qt_b[i][:, lo:lo + W], start=True, stop=not diag),
                            inc=None if diag else ("pe", 1), waits=smw)
                        if diag:
                            msk = tri_bf if kpos < 16 else maskb_bf
                            sm = P.op("tensor", lambda te, ci=ci, msk=msk: te.matmul(
                                sc[ci][:, 0:128], id_bf[:], msk[:], start=False, stop=True),
                                inc=("pe", 1), waits=[])
                        if pending is not None:
                            emit_pvs(pending)
                        ew = [sm, m4e, pt_free[ci]]
                        ee = P.op("scalar", lambda s, ci=ci, W=W, h=h: s.activation(
                            pt_b[ci][:, 0:W], sc[ci][:, 0:W], AF.Exp, bias=negM[:, h % 2:h % 2 + 1]),
                            inc=("act", 1), waits=ew)
                        exp_evs.append(ee)
                        sc_free[ci] = ee
                        pending = (ci, lo, hi, kpos, ee)
                emit_pvs(pending)
                pending = None
                head_attn_done[h] = lastPV[15]
                ast = ast_b[i]
                nrm_ev = None
                tr_ev = {}
                grp_evac = {}
                for s_ in range(16):
                    bank, off = slot_ap(s_)
                    rce = P.op("vector", lambda v, bank=bank, off=off, s_=s_: v.reciprocal(
                        rc_b[:, s_:s_ + 1], bank[:, off + 128:off + 129]),
                        inc=("dve", 1), waits=[lastPV[16 + s_ - 16] if False else lastPV[s_]])
                    nrm_ev = P.op("scalar", lambda s, bank=bank, off=off, s_=s_, i=i: s.activation(
                        at_b[s_ % 2][:], bank[:, off:off + 128], AF.Copy,
                        scale=rc_b[:, s_:s_ + 1]),
                        inc=("act", 1), waits=[rce, tr_ev.get(s_ - 2)])
                    trw = [nrm_ev]
                    if s_ in (0, 4):
                        trw.append(exp_evs[-1])
                    if s_ in (8, 12):
                        trw.append(grp_evac[s_ // 4 - 2])
                    tr_ev[s_] = P.op("tensor", lambda te, s_=s_, i=i: te.matmul(
                        sc[(s_ // 4) % 2][:, 128 * (s_ % 4):128 * (s_ % 4) + 128],
                        at_b[s_ % 2][:], id_bf[:], start=True, stop=True),
                        inc=("pe", 1), waits=trw)
                    if s_ % 4 == 3:
                        gq_ = s_ // 4
                        grp_evac[gq_] = P.op("scalar", lambda s, gq_=gq_, ast=ast: s.activation(
                            ast[:, 512 * gq_:512 * (gq_ + 1)], sc[gq_ % 2][:], AF.Copy),
                            inc=("act", 1), waits=[tr_ev[4 * gq_ + 3], ast_dma.get(h - 2)])
                head_evac_done[h] = nrm_ev
                head_trev_done[h] = grp_evac[3]
                ast_dma[h] = P.dma("scalar", attnT_s[h], ast[:], waits=[grp_evac[3]])
            if "attnT" in dbg:
                aq_all = ("aq", P.counts["aq"])
                for h in range(H):
                    P.dma("scalar", dbg["attnT"][h], attnT_s[h], waits=[aq_all])
            with nc.Block() as blk:
                P.emit(blk)

        P.new_block()
        with contextlib.ExitStack() as ph:
            psb = lambda n, s, dt=F32: ph.enter_context(nc.sbuf_tensor(n, s, dt))
            pps = lambda n, s: ph.enter_context(nc.psum_tensor(n, s, F32))
            xo_sb = psb("p4xo", [128, 16 * 2048], BF16)
            wg_sl = [psb(f"p4wg{i}", [128, 16 * 512], BF16) for i in range(2)]
            wu_sl = [psb(f"p4wu{i}", [128, 16 * 512], BF16) for i in range(2)]
            silu_b = [psb(f"p4si{i}", [128, 1024], BF16) for i in range(2)]
            gat_b = [psb(f"p4ga{i}", [128, 2048], BF16) for i in range(2)]
            ps_g = [pps(f"p4pg{i}", [128, 1024]) for i in range(2)]
            ps_u = [pps(f"p4pu{i}", [128, 1024]) for i in range(2)]

            p3_aq = ("aq", P.counts["aq"])
            p3_sq = ("sq", P.counts["sq"])
            xo_ev = None
            for c in range(16):
                xo_ev = P.dma("sync", xo_sb[:, 2048 * c:2048 * (c + 1)],
                              xnT_s[c][:, 0:2048], waits=[p3_aq])
            sl_ld = {}
            sl_used = {}

            def load_slab(sl):
                ev = None
                for j, (wt, dst) in enumerate(((wgT, wg_sl[sl % 2]), (wuT, wu_sl[sl % 2]))):
                    src = wt[:, 512 * sl:512 * (sl + 1)].rearrange("(a p) w -> p a w", p=128)
                    d3 = dst[:].rearrange("p (c w) -> p c w", w=512)
                    for half in range(2):
                        ev = P.dma("gpsimd", d3[:, 8 * half:8 * (half + 1), :],
                                   src[:, 8 * half:8 * (half + 1), :],
                                   waits=[sl_used.get(sl - 2), consts_gq, p3_sq])
                sl_ld[sl] = ev

            load_slab(0)
            gdma = {}
            evac_ev = {}
            for hc in range(64):
                sl, hl = divmod(hc, 4)
                if hl == 0 and sl + 1 < 16:
                    load_slab(sl + 1)
                gb = gat_b[hc % 2]
                for hf in range(2):
                    pg, pu = ps_g[hf], ps_u[hf]
                    mm = None
                    for c in range(16):
                        for m2 in range(2):
                            w0 = [sl_ld[sl], xo_ev, evac_ev.get((hc - 1, hf))] if (c == 0 and m2 == 0) else []
                            mm = P.op("tensor", lambda te, pg=pg, c=c, m2=m2, sl=sl, hl=hl, hf=hf: te.matmul(
                                pg[:, 512 * m2:512 * (m2 + 1)],
                                wg_sl[sl % 2][:, 512 * c + 128 * hl:512 * c + 128 * (hl + 1)],
                                xo_sb[:].rearrange("p (c w) -> p c w", w=2048)[:, c, 1024 * hf + 512 * m2:1024 * hf + 512 * (m2 + 1)],
                                start=(c == 0), stop=(c == 15)),
                                inc=None, waits=w0)
                    for c in range(16):
                        for m2 in range(2):
                            mm = P.op("tensor", lambda te, pu=pu, c=c, m2=m2, sl=sl, hl=hl, hf=hf: te.matmul(
                                pu[:, 512 * m2:512 * (m2 + 1)],
                                wu_sl[sl % 2][:, 512 * c + 128 * hl:512 * c + 128 * (hl + 1)],
                                xo_sb[:].rearrange("p (c w) -> p c w", w=2048)[:, c, 1024 * hf + 512 * m2:1024 * hf + 512 * (m2 + 1)],
                                start=(c == 0), stop=(c == 15)),
                                inc=("pe", 1) if (c == 15 and m2 == 1) else None, waits=[])
                    sl_used[sl] = mm
                    sie = P.op("scalar", lambda s, pg=pg, hf=hf, hc=hc: s.activation(
                        silu_b[hf][:], pg[:], AF.Silu),
                        inc=("act", 1), waits=[mm])
                    evac_ev[(hc, hf)] = P.op("vector", lambda v, gb=gb, hf=hf, pu=pu: v.tensor_tensor(
                        gb[:, 1024 * hf:1024 * (hf + 1)], silu_b[hf][:], pu[:], ALU.mult),
                        inc=("dve", 1), waits=[sie, gdma.get(hc - 2)])
                gdma[hc] = P.dma("scalar", gatedT_s[hc], gb[:],
                                 waits=[evac_ev[(hc, 0)], evac_ev[(hc, 1)]])
            if "gatedT" in dbg:
                aq_all = ("aq", P.counts["aq"])
                for hc in range(64):
                    P.dma("scalar", dbg["gatedT"][hc], gatedT_s[hc], waits=[aq_all])
            with nc.Block() as blk:
                P.emit(blk)

        P.new_block()
        with contextlib.ExitStack() as ph:
            psb = lambda n, s, dt=F32: ph.enter_context(nc.sbuf_tensor(n, s, dt))
            pps = lambda n, s: ph.enter_context(nc.psum_tensor(n, s, F32))
            at_sb = psb("p5at", [128, 16 * 2048], BF16)
            gt_sb = [psb(f"p5gt{i}", [128, 32 * 256], BF16) for i in range(3)]
            wq_b = [psb(f"p5wq{i}", [128, 4 * 2048], BF16) for i in range(4)]
            scrap_sb = psb("p5scrap", [128, 16])
            y_b = [psb(f"p5y{i}", [128, 2048]) for i in range(2)]
            ps_y = [pps(f"p5py{i}", [128, 2048]) for i in range(2)]

            at_ev = None
            for hh in range(16):
                at_ev = P.dma("sync", at_sb[:, 2048 * hh:2048 * (hh + 1)], attnT_s[hh],
                              waits=[ast_dma.get(hh)])
            gt_ev = {}
            wd_used = {}
            y_dma = {}
            yev = {}

            def load_gt(sp, half):
                ev = None
                gts = gt_sb[(2 * sp + half) % 3]
                d3 = gts[:].rearrange("p (c w) -> p c w", w=256)
                for jj in range(32):
                    hc = 32 * half + jj
                    ev = P.dma("sync", d3[:, jj, :], gatedT_s[hc][:, 256 * sp:256 * (sp + 1)],
                               waits=[gdma[hc] if sp == 0 else None,
                                      gt_used.get((2 * sp + half) - 3)])
                gt_ev[(sp, half)] = ev

            WDQ = {0: "sync", 1: "gpsimd", 2: "scalar"}
            NQUAD = 20

            def load_quad(sp, qd):
                if qd < 16:
                    s2 = wd_bf_s[512 * qd:512 * (qd + 1), :]
                    cwait = conv_st.get(4 * qd + 3)
                else:
                    s2 = wo_bf_s[512 * (qd - 16):512 * (qd - 15), :]
                    cwait = conv_st.get(64 + 4 * (qd - 16) + 3)
                src3 = s2.rearrange("(a p) w -> p a w", p=128)
                dst3 = wq_b[qd % 4][:].rearrange("p (a w) -> p a w", w=2048)
                prev_same_buf = (qd % 3) + 3 * ((NQUAD - 1 - (qd % 3)) // 3)
                return P.dma(WDQ[qd % 3], dst3, src3,
                             waits=[cwait,
                                    wd_used.get((sp, qd - 3)) or wd_used.get((sp - 1, prev_same_buf))])

            gt_used = {}
            load_gt(0, 0)
            load_gt(0, 1)
            load_gt(1, 0)
            wd_ld = {}
            x_ld = {}
            for sp in range(8):
                mme = {0: None, 1: None}
                for ti in range(2):
                    t = 2 * sp + ti
                    x_ld[(sp, ti)] = P.dma("gpsimd", y_b[ti][:], x_perm[128 * t:128 * (t + 1), :],
                                           waits=[y_dma.get((sp - 1, ti))])
                if sp == 0:
                    for pre in range(2):
                        wd_ld[(sp, pre)] = load_quad(sp, pre)
                for hc in range(64):
                    qd = hc // 4
                    if hc % 4 == 0 and qd + 2 < NQUAD:
                        wd_ld[(sp, qd + 2)] = load_quad(sp, qd + 2)
                    wde = wd_ld[(sp, qd)]
                    half = hc // 32
                    gts = gt_sb[(2 * sp + half) % 3]
                    for ti in range(2):
                        for m4 in range(4):
                            mme[ti] = P.op("tensor", lambda te, ti=ti, hc=hc, m4=m4, gts=gts, sp=sp: te.matmul(
                                ps_y[ti][:, 512 * m4:512 * (m4 + 1)],
                                gts[:, 256 * (hc % 32) + 128 * ti:256 * (hc % 32) + 128 * (ti + 1)],
                                wq_b[(hc // 4) % 4][:, 2048 * (hc % 4) + 512 * m4:2048 * (hc % 4) + 512 * (m4 + 1)],
                                start=(hc == 0), stop=False),
                                inc=("pe", 1) if (ti == 1 and m4 == 3) else None,
                                waits=([wde, gt_ev[(sp, half)], yev.get((sp - 1, ti))] if m4 == 0 else []))
                    if hc % 4 == 3:
                        wd_used[(sp, hc // 4)] = mme[1]
                    if hc % 32 == 31:
                        gt_used[2 * sp + half] = mme[1]
                    if hc == 32 and sp + 1 < 8:
                        load_gt(sp + 1, 1)
                for dc in range(16):
                    ch = 64 + dc
                    qd = ch // 4
                    if dc == 0 and sp + 2 < 8:
                        load_gt(sp + 2, 0)
                    if dc == 4 and sp + 1 < 8:
                        wd_ld[(sp + 1, 0)] = load_quad(sp + 1, 0)
                    if dc == 8 and sp + 1 < 8:
                        wd_ld[(sp + 1, 1)] = load_quad(sp + 1, 1)
                    if dc % 4 == 0 and qd + 2 < NQUAD:
                        wd_ld[(sp, qd + 2)] = load_quad(sp, qd + 2)
                    woe = wd_ld[(sp, qd)]
                    woe2 = None
                    for ti in range(2):
                        t = 2 * sp + ti
                        for m4 in range(4):
                            mme[ti] = P.op("tensor", lambda te, ti=ti, dc=dc, m4=m4, t=t: te.matmul(
                                ps_y[ti][:, 512 * m4:512 * (m4 + 1)],
                                at_sb[:, 2048 * dc + 128 * t:2048 * dc + 128 * (t + 1)],
                                wq_b[((64 + dc) // 4) % 4][:, 2048 * (dc % 4) + 512 * m4:2048 * (dc % 4) + 512 * (m4 + 1)],
                                start=False, stop=False),
                                inc=("pe", 1) if (ti == 1 and m4 == 3) else None,
                                waits=([woe, woe2, at_ev] if m4 == 0 else []))
                    if dc % 4 == 3:
                        wd_used[(sp, (64 + dc) // 4)] = mme[1]
                for ti in range(2):
                    t = 2 * sp + ti
                    for m4 in range(4):
                        mme[ti] = P.op("tensor", lambda te, ti=ti, m4=m4: te.matmul(
                            ps_y[ti][:, 512 * m4:512 * (m4 + 1)],
                            onesr_bf[0:1, :], bo_bf[0:1, 512 * m4:512 * (m4 + 1)],
                            start=False, stop=True),
                            inc=("pe", 1) if m4 == 3 else None, waits=[])
                    pass
                for ti in range(2):
                    t = 2 * sp + ti
                    P.op("vector", lambda v, ti=ti: v.tensor_copy(scrap_sb[:, 0:16], ps_y[ti][:, 0:16]),
                         waits=[mme[1], x_ld[(sp, ti)]])
                    ye = P.op("vector", lambda v, ti=ti: v.tensor_tensor(
                        y_b[ti][:], ps_y[ti][:], y_b[ti][:], ALU.add),
                        inc=("dve", 1))
                    yev[(sp, ti)] = ye
                    y_dma[(sp, ti)] = P.dma("scalar", y_out[128 * t:128 * (t + 1), :], y_b[ti][:],
                                            waits=[ye])
            if "at_sb" in dbg:
                last_y = y_dma[(7, 1)]
                P.dma("scalar", dbg["at_sb"], at_sb[:], waits=[last_y])
                P.dma("scalar", dbg["gt_sb"], gt_sb[(2 * 7 + 1) % 3][:], waits=[last_y])
            P.op("gpsimd", lambda e: e.wait_ge(sems["aq"], P.counts["aq"]))
            P.op("gpsimd", lambda e: e.wait_ge(sems["sq"], P.counts["sq"]))
            P.op("gpsimd", lambda e: e.wait_ge(sems["gq"], P.counts["gq"]))
            with nc.Block() as blk:
                P.emit(blk)

    return nc


_NC_CACHE = {}


def get_nc():
    key = tuple(DBG)
    if key not in _NC_CACHE:
        _NC_CACHE[key] = build_nc()
    return _NC_CACHE[key]


def make_inputs(L0_attn_x, L0_attn_w_qkv, L0_attn_b_qkv, L0_attn_w_out, L0_attn_b_out,
                L0_ffn_w_gate, L0_ffn_w_up, L0_ffn_w_down):
    x = np.asarray(L0_attn_x, dtype=np.float32)
    w_qkv = np.asarray(L0_attn_w_qkv, dtype=np.float32)
    b_qkv = np.asarray(L0_attn_b_qkv, dtype=np.float32)
    w_out = np.asarray(L0_attn_w_out, dtype=np.float32)
    b_out = np.asarray(L0_attn_b_out, dtype=np.float32)
    w_gate = np.asarray(L0_ffn_w_gate, dtype=np.float32)
    w_up = np.asarray(L0_ffn_w_up, dtype=np.float32)
    w_down = np.asarray(L0_ffn_w_down, dtype=np.float32)

    shared = {
        "wqT": np.ascontiguousarray(w_qkv[0:D].T),
        "wkT": np.ascontiguousarray(w_qkv[D:2 * D].T),
        "wvT": np.ascontiguousarray(w_qkv[2 * D:3 * D].T),
        "woT": np.ascontiguousarray(w_out.T),
        "wgT": np.ascontiguousarray(w_gate.T),
        "wuT": np.ascontiguousarray(w_up.T),
        "wdT": np.ascontiguousarray(w_down.T),
        "bq_t": np.ascontiguousarray(b_qkv[0:D].reshape(16, 128).T),
        "bk_t": np.ascontiguousarray(b_qkv[D:2 * D].reshape(16, 128).T),
        "bv_row": np.ascontiguousarray(b_qkv[2 * D:3 * D].reshape(1, D)),
        "bo_row": np.ascontiguousarray(b_out.reshape(1, D)),
        "tri_m": np.where(np.arange(128)[:, None] > np.arange(128)[None, :],
                          np.float32(-1e30), np.float32(0.0)),
        "ident": np.eye(128, dtype=np.float32),
        "ones_col": np.ones((128, 1), dtype=np.float32),
        "ones_row": np.ones((1, 128), dtype=np.float32),
        "neg_ones_row": -np.ones((1, 128), dtype=np.float32),
    }
    in_maps = []
    perms = []
    for c in range(8):
        b, h = divmod(c, 2)
        own = [2 * i + h for i in range(16)]
        oth = [2 * i + (1 - h) for i in range(16)]
        perm = own + oth
        perms.append(own)
        xb = x[b].reshape(NT, 128, D)[perm].reshape(S, D)
        mb = np.full((128, 128), -1e30, np.float32) if h == 0 else np.zeros((128, 128), np.float32)
        m = dict(shared)
        m["x_perm"] = np.ascontiguousarray(xb)
        m["mask_b"] = mb
        in_maps.append(m)
    return in_maps, perms


def kernel(L0_attn_x, L0_attn_w_qkv, L0_attn_b_qkv, L0_attn_w_out, L0_attn_b_out,
           L0_ffn_w_gate, L0_ffn_w_up, L0_ffn_w_down):
    in_maps, perms = make_inputs(L0_attn_x, L0_attn_w_qkv, L0_attn_b_qkv,
                                 L0_attn_w_out, L0_attn_b_out,
                                 L0_ffn_w_gate, L0_ffn_w_up, L0_ffn_w_down)
    nc = get_nc()
    res = run_bass_kernel_spmd(nc, in_maps, core_ids=list(range(8)))
    out = np.empty((4, S, D), dtype=np.float32)
    for c in range(8):
        b = c // 2
        yc = res.results[c]["y"].reshape(16, 128, D)
        for i, gt in enumerate(perms[c]):
            out[b, 128 * gt:128 * (gt + 1), :] = yc[i]
    return out


# version 14
# speedup vs baseline: 1.1508x; 1.1508x over previous
import contextlib
import math
import numpy as np
import ml_dtypes
import concourse.bass as bass
import concourse.mybir as mybir
from concourse.bass_utils import run_bass_kernel_spmd

F32 = mybir.dt.float32
BF16 = mybir.dt.bfloat16
AF = mybir.ActivationFunctionType
ALU = mybir.AluOpType
AX = mybir.AxisListType

D = 2048
H = 16
HD = 128
S = 4096
DH = 8192
NT = 32
NO = 16
INV_SQRT_HD = 1.0 / math.sqrt(HD)

ENGS = ("sync", "gpsimd", "vector", "scalar", "tensor")
DBG = ()


class Plan:
    def __init__(self, sems):
        self.sems = sems
        self.counts = {k: 0 for k in sems}
        self.ops = {e: [] for e in ENGS}
        self.waited = {e: {k: 0 for k in sems} for e in ENGS}

    def new_block(self):
        self.ops = {e: [] for e in ENGS}

    def op(self, eng, fn, inc=None, waits=()):
        for w in waits:
            if w is None:
                continue
            s, v = w
            if v > self.waited[eng][s]:
                self.waited[eng][s] = v
                self.ops[eng].append(
                    lambda e, _s=self.sems[s], _v=v: e.wait_ge(_s, _v))
        if inc is None:
            self.ops[eng].append(fn)
            return None
        s, n = inc
        self.counts[s] += n
        self.ops[eng].append(
            lambda e, _f=fn, _s=self.sems[s], _n=n: _f(e).then_inc(_s, _n))
        return (s, self.counts[s])

    def dma(self, eng, out, in_, waits=(), sem=None):
        if sem is None:
            sem = {"sync": "sq", "gpsimd": "gq", "scalar": "aq"}[eng]
        return self.op(eng, lambda e, _o=out, _i=in_: e.dma_start(out=_o, in_=_i),
                       inc=(sem, 16), waits=waits)

    def emit(self, block):
        for eng in ENGS:
            lst = list(self.ops[eng])
            if not lst:
                continue
            def body(e, _lst=lst):
                for f in _lst:
                    f(e)
            getattr(block, eng)(body)


def build_nc():
    nc = bass.Bass("TRN2", target_bir_lowering=False)

    x_perm = nc.dram_tensor("x_perm", [S, D], F32, kind="ExternalInput").ap()
    wqT = nc.dram_tensor("wqT", [D, D], F32, kind="ExternalInput").ap()
    wkT = nc.dram_tensor("wkT", [D, D], F32, kind="ExternalInput").ap()
    wvT = nc.dram_tensor("wvT", [D, D], F32, kind="ExternalInput").ap()
    woT = nc.dram_tensor("woT", [D, D], F32, kind="ExternalInput").ap()
    wgT = nc.dram_tensor("wgT", [D, DH], F32, kind="ExternalInput").ap()
    wuT = nc.dram_tensor("wuT", [D, DH], F32, kind="ExternalInput").ap()
    wdT = nc.dram_tensor("wdT", [DH, D], F32, kind="ExternalInput").ap()
    bq_t = nc.dram_tensor("bq_t", [128, 16], F32, kind="ExternalInput").ap()
    bk_t = nc.dram_tensor("bk_t", [128, 16], F32, kind="ExternalInput").ap()
    bv_row = nc.dram_tensor("bv_row", [1, D], F32, kind="ExternalInput").ap()
    bo_row = nc.dram_tensor("bo_row", [1, D], F32, kind="ExternalInput").ap()
    tri_m = nc.dram_tensor("tri_m", [128, 128], F32, kind="ExternalInput").ap()
    mask_b = nc.dram_tensor("mask_b", [128, 128], F32, kind="ExternalInput").ap()
    ident = nc.dram_tensor("ident", [128, 128], F32, kind="ExternalInput").ap()
    ones_col = nc.dram_tensor("ones_col", [128, 1], F32, kind="ExternalInput").ap()
    ones_row = nc.dram_tensor("ones_row", [1, 128], F32, kind="ExternalInput").ap()
    neg_ones_row = nc.dram_tensor("neg_ones_row", [1, 128], F32, kind="ExternalInput").ap()

    y_out = nc.dram_tensor("y", [NO * 128, D], F32, kind="ExternalOutput").ap()

    xnT_s = [nc.dram_tensor(f"xnT_s{c}", [128, S], BF16, kind="Internal").ap() for c in range(16)]
    qT_s = [nc.dram_tensor(f"qT_s{h}", [128, 2048], BF16, kind="Internal").ap() for h in range(H)]
    kT_s = [nc.dram_tensor(f"kT_s{h}", [128, S], BF16, kind="Internal").ap() for h in range(H)]
    v_s = [nc.dram_tensor(f"v_s{h}", [S, HD], BF16, kind="Internal").ap() for h in range(H)]
    attnT_s = [nc.dram_tensor(f"attnT_s{h}", [128, 2048], BF16, kind="Internal").ap() for h in range(H)]
    gatedT_s = [nc.dram_tensor(f"gatedT_s{hc}", [128, 2048], BF16, kind="Internal").ap() for hc in range(64)]
    wd_bf_s = nc.dram_tensor("wd_bf_s", [DH, D], BF16, kind="Internal").ap()
    wo_bf_s = nc.dram_tensor("wo_bf_s", [D, D], BF16, kind="Internal").ap()

    dbg = {}
    if "xnT" in DBG:
        dbg["xnT"] = [nc.dram_tensor(f"dbg_xnT{c}", [128, S], BF16, kind="ExternalOutput").ap() for c in range(16)]
    if "qkv" in DBG:
        dbg["q"] = [nc.dram_tensor(f"dbg_q{h}", [128, 2048], BF16, kind="ExternalOutput").ap() for h in range(H)]
        dbg["k"] = [nc.dram_tensor(f"dbg_k{h}", [128, S], BF16, kind="ExternalOutput").ap() for h in range(H)]
        dbg["v"] = [nc.dram_tensor(f"dbg_v{h}", [S, HD], BF16, kind="ExternalOutput").ap() for h in range(H)]
        dbg["nq"] = nc.dram_tensor("dbg_nq", [1, 16], F32, kind="ExternalOutput").ap()
        dbg["nk"] = nc.dram_tensor("dbg_nk", [1, 16], F32, kind="ExternalOutput").ap()
    if "attnT" in DBG:
        dbg["attnT"] = [nc.dram_tensor(f"dbg_attnT{h}", [128, 2048], BF16, kind="ExternalOutput").ap() for h in range(H)]
    if "gatedT" in DBG:
        dbg["gatedT"] = [nc.dram_tensor(f"dbg_gatedT{hc}", [128, 2048], BF16, kind="ExternalOutput").ap() for hc in range(64)]
    if "p5in" in DBG:
        dbg["at_sb"] = nc.dram_tensor("dbg_at_sb", [128, 16 * 2048], BF16, kind="ExternalOutput").ap()
        dbg["gt_sb"] = nc.dram_tensor("dbg_gt_sb", [128, 32 * 256], BF16, kind="ExternalOutput").ap()

    top = contextlib.ExitStack()
    sbuf = lambda name, shape, dt=F32: top.enter_context(nc.sbuf_tensor(name, shape, dt))

    with top:
        sems = {}
        for name in ("sq", "gq", "aq", "pe", "act", "dve",
                     "hq0", "hq1", "wl0", "wl1", "wl2", "wl3",
                     "gl0", "gl1", "gl2", "xl0", "xl1"):
            sems[name] = top.enter_context(nc.semaphore(name))
        P = Plan(sems)

        tri_sb = sbuf("tri_sb", [128, 128])
        maskb_sb = sbuf("maskb_sb", [128, 128])
        tri_bf = sbuf("tri_bf", [128, 128], BF16)
        maskb_bf = sbuf("maskb_bf", [128, 128], BF16)
        id_bf = sbuf("id_bf", [128, 128], BF16)
        onesc_f = sbuf("onesc_f", [128, 1])
        onesr_bf = sbuf("onesr_bf", [1, 128], BF16)
        negonesr_f = sbuf("negonesr_f", [1, 128])
        bqs_sb = sbuf("bqs_sb", [128, 16])
        bk_sb = sbuf("bk_sb", [128, 16])
        bv_bf = sbuf("bv_bf", [1, D], BF16)
        bo_bf = sbuf("bo_bf", [1, D], BF16)
        c_invD = sbuf("c_invD", [128, 1])
        c_eps = sbuf("c_eps", [128, 1])
        c_isqhd = sbuf("c_isqhd", [128, 1])
        c_mscale = sbuf("c_mscale", [1, 1])
        nq_sb = sbuf("nq_sb", [1, H])
        nk_sb = sbuf("nk_sb", [1, H])

        P.new_block()
        P.dma("sync", tri_sb[:], tri_m)
        P.dma("sync", maskb_sb[:], mask_b)
        P.dma("sync", negonesr_f[:], neg_ones_row)
        P.dma("sync", onesc_f[:], ones_col)
        P.dma("sync", bk_sb[:], bk_t)
        bq_ev = P.dma("sync", bqs_sb[:], bq_t)
        P.dma("gpsimd", id_bf[:], ident)
        P.dma("gpsimd", tri_bf[:], tri_m)
        P.dma("gpsimd", maskb_bf[:], mask_b)
        P.dma("gpsimd", onesr_bf[:], ones_row)
        P.dma("gpsimd", bv_bf[:], bv_row)
        consts_gq = P.dma("gpsimd", bo_bf[:], bo_row)
        P.op("vector", lambda v: v.memset(c_invD[:], 1.0 / D))
        P.op("vector", lambda v: v.memset(c_eps[:], 1e-5))
        P.op("vector", lambda v: v.memset(c_mscale[:], 4.0 / (float(D) * float(S))))
        ms_ev = P.op("vector", lambda v: v.memset(c_isqhd[:], INV_SQRT_HD), inc=("dve", 1))
        bqs_ev = P.op("vector",
                      lambda v: v.tensor_scalar(bqs_sb[:], bqs_sb[:], c_isqhd[:, 0:1], None, ALU.mult),
                      inc=("dve", 1), waits=[bq_ev, ms_ev])
        consts_sq = bq_ev
        with nc.Block() as blk:
            P.emit(blk)

        P.new_block()
        grp_dma = {}
        with contextlib.ExitStack() as ph:
            psb = lambda n, s, dt=F32: ph.enter_context(nc.sbuf_tensor(n, s, dt))
            pps = lambda n, s: ph.enter_context(nc.psum_tensor(n, s, F32))
            x_sb = [psb(f"p1x{i}", [128, D]) for i in range(3)]
            xsq_sb = psb("p1sq", [128, D], BF16)
            stat = [psb(f"p1st{i}", [128, 8]) for i in range(2)]
            xn_bf = [psb(f"p1xn{i}", [128, D], BF16) for i in range(2)]
            stage = [psb(f"p1stage{i}", [128, 16 * 512], BF16) for i in range(2)]
            ps_tr = [pps(f"p1tr{i}", [128, 512]) for i in range(2)]

            xn_done, evac_done, sq_done = {}, {}, {}
            for t in range(NT):
                g, tt = divmod(t, 4)
                xb, stt, xnb, stg = x_sb[t % 3], stat[t % 2], xn_bf[t % 2], stage[g % 2]
                x_ev = P.dma("sync", xb[:], x_perm[128 * t:128 * (t + 1), :],
                             waits=[xn_done.get(t - 3)])
                s1 = P.op("vector", lambda v, xb=xb, stt=stt: v.reduce_sum(
                    stt[:, 0:1], xb[:], axis=AX.X),
                    inc=("dve", 1), waits=[x_ev, xn_done.get(t - 2)])
                sq_ev = P.op("scalar", lambda s, xb=xb, stt=stt: s.activation(
                    xsq_sb[:], xb[:], AF.Square, accum_out=stt[:, 1:2]),
                    inc=("act", 1), waits=[x_ev, xn_done.get(t - 2), sq_done.get(t - 1)])
                sq_done[t] = sq_ev
                c1 = P.op("vector", lambda v, stt=stt: v.tensor_scalar(
                    stt[:, 2:3], stt[:, 0:1], c_invD[:, 0:1], None, ALU.mult),
                    inc=("dve", 1), waits=[s1])
                c3 = P.op("vector", lambda v, stt=stt: v.tensor_tensor(
                    stt[:, 4:5], stt[:, 2:3], stt[:, 2:3], ALU.mult),
                    inc=("dve", 1), waits=[c1])
                c4 = P.op("vector", lambda v, stt=stt: v.scalar_tensor_tensor(
                    stt[:, 5:6], stt[:, 1:2], c_invD[:, 0:1], stt[:, 4:5], ALU.mult, ALU.subtract),
                    inc=("dve", 1), waits=[c3, sq_ev])
                c5 = P.op("vector", lambda v, stt=stt: v.tensor_tensor(
                    stt[:, 5:6], stt[:, 5:6], c_eps[:], ALU.add),
                    inc=("dve", 1), waits=[c4])
                c6 = P.op("scalar", lambda s, stt=stt: s.sqrt(stt[:, 6:7], stt[:, 5:6]),
                          inc=("act", 1), waits=[c5])
                c7 = P.op("vector", lambda v, stt=stt: v.reciprocal(stt[:, 6:7], stt[:, 6:7]),
                          inc=("dve", 1), waits=[c6])
                c8 = P.op("vector", lambda v, stt=stt: v.scalar_tensor_tensor(
                    stt[:, 7:8], stt[:, 2:3], -1.0, stt[:, 6:7], ALU.mult, ALU.mult),
                    inc=("dve", 1), waits=[c7])
                xn_ev = P.op("scalar", lambda s, xb=xb, xnb=xnb, stt=stt: s.activation(
                    xnb[:], xb[:], AF.Identity, bias=stt[:, 7:8], scale=stt[:, 6:7]),
                    inc=("act", 1), waits=[c8, evac_done.get((t - 2, 3))])
                xn_done[t] = xn_ev
                for pg in range(4):
                    pb = ps_tr[pg % 2]
                    prev_ev = evac_done.get((t, pg - 2)) or evac_done.get((t - 1, pg + 2))
                    tr_ev = None
                    for i in range(4):
                        c = 4 * pg + i
                        tr_ev = P.op("tensor", lambda te, xnb=xnb, pb=pb, c=c, i=i: te.matmul(
                            pb[:, 128 * i:128 * (i + 1)],
                            xnb[:, 128 * c:128 * (c + 1)], id_bf[:],
                            start=True, stop=True),
                            inc=("pe", 1) if i == 3 else None,
                            waits=([xn_ev, prev_ev, consts_gq] if i == 0 else []))
                    stg3 = stg[:].rearrange("p (c w) -> p c w", w=512)
                    pb3 = pb[:].rearrange("p (c w) -> p c w", w=128)
                    ev = P.op("scalar", lambda s, stg3=stg3, pb3=pb3, pg=pg, tt=tt: s.activation(
                        stg3[:, 4 * pg:4 * pg + 4, 128 * tt:128 * (tt + 1)], pb3[:, :, :], AF.Copy),
                        inc=("act", 1), waits=[tr_ev, grp_dma.get(g - 2)])
                    evac_done[(t, pg)] = ev
                if tt == 3:
                    last_ev = evac_done[(t, 3)]
                    for c in range(16):
                        grp_dma[g] = P.dma("scalar", xnT_s[c][:, 512 * g:512 * (g + 1)],
                                           stg[:, 512 * c:512 * (c + 1)], waits=[last_ev])
            if "xnT" in dbg:
                aq_all = ("aq", P.counts["aq"])
                for c in range(16):
                    P.dma("scalar", dbg["xnT"][c], xnT_s[c], waits=[aq_all])
            with nc.Block() as blk:
                P.emit(blk)

        P.new_block()
        with contextlib.ExitStack() as ph:
            psb = lambda n, s, dt=F32: ph.enter_context(nc.sbuf_tensor(n, s, dt))
            pps = lambda n, s: ph.enter_context(nc.psum_tensor(n, s, F32))
            w_sl = [[psb(f"p2w{j}{i}", [128, 16 * 512], BF16) for i in range(2)] for j in range(3)]
            xg_sb = [psb(f"p2xg{i}", [128, 16 * 512], BF16) for i in range(2)]
            kq_buf = [psb(f"p2kq{i}", [128, 512], BF16) for i in range(4)]
            v_buf = [psb(f"p2v{i}", [128, 512], BF16) for i in range(2)]
            sqd_sb = psb("p2sqd", [128, 512], BF16)
            nacc = psb("p2nacc", [128, 2])
            ps_kq = [pps(f"p2pkq{i}", [128, 512]) for i in range(2)]
            ps_v = [pps(f"p2pv{i}", [128, 512]) for i in range(2)]
            ps_n_full = pps("p2pn", [128, 512])

            w_ld = {}
            slab_used = {}
            xg_ev = {}
            xg_used = {}
            kq_evac = {}
            kq_dmae = {}
            v_evac = {}
            v_dmae = {}
            last_sq = consts_sq
            last_nmm = {0: None, 1: None}
            n_first = [True]
            n_evt = None
            unit_ctr = [0]

            def load_slabs(hg):
                ev = None
                for j, wt in enumerate((wqT, wkT, wvT)):
                    wsb = w_sl[j][hg % 2]
                    for c in range(16):
                        ev = P.dma("gpsimd", wsb[:, 512 * c:512 * (c + 1)],
                                   wt[128 * c:128 * (c + 1), 512 * hg:512 * (hg + 1)],
                                   waits=[slab_used.get(hg - 2), consts_gq])
                w_ld[hg] = ev

            load_slabs(0)
            for hg in range(4):
                if hg < 3:
                    load_slabs(hg + 1)
                wq_sb, wk_sb, wv_sb = (w_sl[j][hg % 2] for j in range(3))
                for g in range(8):
                    xg = xg_sb[g % 2]
                    xev = None
                    for c in range(16):
                        xev = P.dma("sync", xg[:, 512 * c:512 * (c + 1)],
                                    xnT_s[c][:, 512 * g:512 * (g + 1)],
                                    waits=[grp_dma[g] if hg == 0 else None,
                                           xg_used.get((hg, g - 2)) or xg_used.get((hg - 1, g + 6))])
                    xg_ev[(hg, g)] = xev
                    for hl in range(4):
                        h = 4 * hg + hl
                        for isq in (False, True):
                            if isq and g >= 4:
                                continue
                            wsb = wq_sb if isq else wk_sb
                            scr = qT_s if isq else kT_s
                            u = unit_ctr[0]; unit_ctr[0] += 1
                            pp = ps_kq[u % 2]
                            kqb = kq_buf[u % 4]
                            mm = None
                            for c in range(16):
                                mm = P.op("tensor", lambda te, pp=pp, wsb=wsb, xg=xg, c=c, hl=hl: te.matmul(
                                    pp[:], wsb[:, 512 * c + 128 * hl:512 * c + 128 * (hl + 1)],
                                    xg[:, 512 * c:512 * (c + 1)],
                                    start=(c == 0), stop=(c == 15)),
                                    inc=("pe", 1) if c == 15 else None,
                                    waits=([w_ld[hg], xev, kq_evac.get(id(pp))] if c == 0 else []))
                            slab_used[hg] = mm
                            if isq:
                                ev = P.op("scalar", lambda s, kqb=kqb, pp=pp, h=h: s.activation(
                                    kqb[:], pp[:], AF.Identity, bias=bqs_sb[:, h:h + 1], scale=c_isqhd[:, 0:1]),
                                    inc=("act", 1), waits=[mm, kq_dmae.get(id(kqb)), bqs_ev])
                            else:
                                ev = P.op("scalar", lambda s, kqb=kqb, pp=pp, h=h: s.activation(
                                    kqb[:], pp[:], AF.Identity, bias=bk_sb[:, h:h + 1]),
                                    inc=("act", 1), waits=[mm, kq_dmae.get(id(kqb))])
                            kq_evac[id(pp)] = ev
                            col = 0 if isq else 1
                            sq2 = P.op("scalar", lambda s, kqb=kqb, col=col: s.activation(
                                sqd_sb[:], kqb[:], AF.Square, accum_out=nacc[:, col:col + 1]),
                                inc=("act", 1), waits=[ev, last_nmm[col]])
                            last_sq = sq2
                            stv = n_first[0]; n_first[0] = False
                            ncol = h if isq else 64 + h
                            nmm = P.op("tensor", lambda te, ncol=ncol, stv=stv, col=col: te.matmul(
                                ps_n_full[0:1, ncol:ncol + 1], onesc_f[:], nacc[:, col:col + 1],
                                start=stv, stop=False),
                                inc=("pe", 1), waits=[sq2])
                            last_nmm[col] = nmm
                            n_evt = nmm
                            dm = P.dma("scalar", scr[h][:, 512 * g:512 * (g + 1)], kqb[:], waits=[ev])
                            kq_dmae[id(kqb)] = dm
                    for tt in range(4):
                        t = 4 * g + tt
                        pp = ps_v[tt % 2]
                        vb = v_buf[tt % 2]
                        for c in range(16):
                            P.op("tensor", lambda te, pp=pp, xg=xg, wv_sb=wv_sb, c=c, tt=tt: te.matmul(
                                pp[:], xg[:, 512 * c + 128 * tt:512 * c + 128 * (tt + 1)],
                                wv_sb[:, 512 * c:512 * (c + 1)],
                                start=(c == 0), stop=False),
                                inc=None,
                                waits=([w_ld[hg], xev, v_evac.get(id(pp))] if c == 0 else []))
                        mm = P.op("tensor", lambda te, pp=pp, hg=hg: te.matmul(
                            pp[:], onesr_bf[0:1, :], bv_bf[0:1, 512 * hg:512 * (hg + 1)],
                            start=False, stop=True),
                            inc=("pe", 1))
                        slab_used[hg] = mm
                        xg_used[(hg, g)] = mm
                        ev = P.op("vector", lambda v, vb=vb, pp=pp: v.tensor_copy(vb[:], pp[:]),
                                  inc=("dve", 1), waits=[mm, v_dmae.get(id(vb))])
                        v_evac[id(pp)] = ev
                        for hl in range(4):
                            h = 4 * hg + hl
                            v_dmae[id(vb)] = P.dma("scalar", v_s[h][128 * t:128 * (t + 1), :],
                                                   vb[:, 128 * hl:128 * (hl + 1)], waits=[ev])
            nf_q = P.op("scalar", lambda s: s.copy(nq_sb[:], ps_n_full[0:1, 0:H]),
                        inc=("act", 1), waits=[n_evt])
            nf_k = P.op("scalar", lambda s: s.copy(nk_sb[:], ps_n_full[0:1, 64:64 + H]),
                        inc=("act", 1), waits=[n_evt])
            if "q" in dbg:
                aq_all = ("aq", P.counts["aq"])
                for h in range(H):
                    P.dma("scalar", dbg["q"][h], qT_s[h], waits=[nf_k, aq_all])
                    P.dma("scalar", dbg["k"][h], kT_s[h], waits=[nf_k, aq_all])
                    P.dma("scalar", dbg["v"][h], v_s[h], waits=[nf_k, aq_all])
                P.dma("scalar", dbg["nq"], nq_sb[0:1, 0:H], waits=[nf_k, aq_all])
                P.dma("scalar", dbg["nk"], nk_sb[0:1, 0:H], waits=[nf_k, aq_all])
            with nc.Block() as blk:
                P.emit(blk)
        p2_aq = ("aq", P.counts["aq"])

        P.new_block()
        with contextlib.ExitStack() as ph:
            psb = lambda n, s, dt=F32: ph.enter_context(nc.sbuf_tensor(n, s, dt))
            pps = lambda n, s: ph.enter_context(nc.psum_tensor(n, s, F32))
            kt_b = [psb(f"p3kt{i}", [128, S], BF16) for i in range(2)]
            qt_b = [psb(f"p3qt{i}", [128, 2048], BF16) for i in range(2)]
            v_b = [psb(f"p3v{i}", [128, 32 * 130], BF16) for i in range(2)]
            pt_b = [psb(f"p3pt{i}", [128, 512], BF16) for i in range(2)]
            rc_b = psb("p3rc", [128, 16])
            m1_b = psb("p3m1", [1, 1])
            negM = psb("p3negM", [128, 2])
            at_b = [psb(f"p3at{i}", [128, 128], BF16) for i in range(2)]
            ast_b = [psb(f"p3ast{i}", [128, 2048], BF16) for i in range(2)]
            conv_b = [psb(f"p3cv{i}", [128, 2048], BF16) for i in range(2)]
            acc = [pps(f"p3acc{i}", [128, 512]) for i in range(6)]
            sc = [pps(f"p3sc{i}", [128, 512]) for i in range(2)]

            def slot_ap(s):
                return acc[s // 3 if s < 15 else 5], 170 * (s % 3) if s < 15 else 0

            vm_ev = None
            for i in range(2):
                for t in range(32):
                    vm_ev = P.op("vector", lambda v, i=i, t=t: v.memset(
                        v_b[i][:, 130 * t + 128:130 * t + 129], 1.0),
                        inc=("dve", 1) if (i == 1 and t == 31) else None)

            conv_jobs = [(wd_bf_s[128 * c:128 * (c + 1), :], wdT[128 * c:128 * (c + 1), :]) for c in range(64)]
            conv_jobs += [(wo_bf_s[128 * c:128 * (c + 1), :], woT[128 * c:128 * (c + 1), :]) for c in range(16)]
            conv_st = {}

            m4_prev = [None]
            head_loads = {}
            head_attn_done = {}
            head_evac_done = {}
            head_trev_done = {}
            ast_dma = {}

            def load_head(h):
                i = h % 2
                lane = f"hq{h % 2}"
                w = [head_attn_done.get(h - 2)]
                if h < 2:
                    w.append(p2_aq)
                kev = P.dma("sync", kt_b[i][:], kT_s[h], waits=w, sem=lane)
                qev = P.dma("sync", qt_b[i][:], qT_s[h], waits=w, sem=lane)
                vev = None
                for t in range(32):
                    vev = P.dma("sync", v_b[i][:, 130 * t:130 * t + 128],
                                v_s[h][128 * t:128 * (t + 1), :], waits=w, sem=lane)
                head_loads[h] = vev

            load_head(0)
            for h in range(H):
                i = h % 2
                if h + 1 < H:
                    load_head(h + 1)
                for j in range(5 * h, min(5 * (h + 1), len(conv_jobs))):
                    dst, src = conv_jobs[j]
                    cb = conv_b[j % 2]
                    lev = P.dma("gpsimd", cb[:], src, waits=[conv_st.get(j - 2), consts_gq])
                    conv_st[j] = P.dma("sync", dst, cb[:], waits=[lev])
                mw = [nf_q, nf_k, m4_prev[0]]
                m1e = P.op("vector", lambda v, h=h: v.tensor_tensor(
                    m1_b[:], nq_sb[0:1, h:h + 1], nk_sb[0:1, h:h + 1], ALU.mult),
                    inc=("dve", 1), waits=mw)
                m2e = P.op("scalar", lambda s: s.activation(
                    m1_b[:], m1_b[:], AF.Sqrt, scale=c_mscale[0:1, 0:1]),
                    inc=("act", 1), waits=[m1e])
                m3e = P.op("tensor", lambda te, h=h: te.matmul(
                    acc[5][0:128, 300:301], negonesr_f[0:1, :], m1_b[0:1, 0:1],
                    start=True, stop=True),
                    inc=("pe", 1), waits=[m2e, head_evac_done.get(h - 1)])
                m4e = P.op("scalar", lambda s, h=h: s.copy(negM[:, h % 2:h % 2 + 1], acc[5][:, 300:301]),
                           inc=("act", 1), waits=[m3e])
                m4_prev[0] = m4e

                sc_free = {0: head_trev_done.get(h - 1), 1: head_trev_done.get(h - 1)}
                pt_free = {0: None, 1: None}
                chunk_idx = 0
                lastPV = {}
                exp_evs = []
                pending = None

                def emit_pvs(pend):
                    ci, lo, hi, kpos, ee = pend
                    pvw = [ee]
                    pv = None
                    for s_ in range(lo // 128, hi // 128):
                        bank, off = slot_ap(s_)
                        if kpos == 0 and s_ == 15:
                            pvw = pvw + [m4_prev[0]]
                        pv = P.op("tensor", lambda te, bank=bank, off=off, ci=ci, s_=s_, lo=lo, kpos=kpos, i=i: te.matmul(
                            bank[:, off:off + 129],
                            pt_b[ci][:, 128 * s_ - lo:128 * s_ - lo + 128],
                            v_b[i][:, 130 * kpos:130 * kpos + 129],
                            start=(kpos == 0 and s_ % 3 == 0), stop=(kpos == 16 + s_)),
                            inc=("pe", 1), waits=pvw)
                        pvw = []
                        lastPV[s_] = pv
                    pt_free[ci] = pv

                for kpos in range(32):
                    p = kpos % 16
                    kw = [head_loads[h], vm_ev]
                    if h == 0:
                        kw.append(head_loads.get(1))
                    for blk_i in range(p // 4, 4):
                        lo = max(128 * p, 512 * blk_i)
                        hi = 512 * (blk_i + 1)
                        W = hi - lo
                        ci = chunk_idx % 2
                        chunk_idx += 1
                        smw = kw + [sc_free[ci], head_evac_done.get(h - 1) if kpos == 0 else None]
                        diag = (lo == 128 * p)
                        sm = P.op("tensor", lambda te, ci=ci, kpos=kpos, lo=lo, W=W, i=i, diag=diag: te.matmul(
                            sc[ci][:, 0:W], kt_b[i][:, 128 * kpos:128 * (kpos + 1)],
                            qt_b[i][:, lo:lo + W], start=True, stop=not diag),
                            inc=None if diag else ("pe", 1), waits=smw)
                        if diag:
                            msk = tri_bf if kpos < 16 else maskb_bf
                            sm = P.op("tensor", lambda te, ci=ci, msk=msk: te.matmul(
                                sc[ci][:, 0:128], id_bf[:], msk[:], start=False, stop=True),
                                inc=("pe", 1), waits=[])
                        if pending is not None:
                            emit_pvs(pending)
                        ew = [sm, m4e, pt_free[ci]]
                        ee = P.op("scalar", lambda s, ci=ci, W=W, h=h: s.activation(
                            pt_b[ci][:, 0:W], sc[ci][:, 0:W], AF.Exp, bias=negM[:, h % 2:h % 2 + 1]),
                            inc=("act", 1), waits=ew)
                        exp_evs.append(ee)
                        sc_free[ci] = ee
                        pending = (ci, lo, hi, kpos, ee)
                emit_pvs(pending)
                pending = None
                head_attn_done[h] = lastPV[15]
                ast = ast_b[i]
                nrm_ev = None
                tr_ev = {}
                grp_evac = {}
                for s_ in range(16):
                    bank, off = slot_ap(s_)
                    rce = P.op("vector", lambda v, bank=bank, off=off, s_=s_: v.reciprocal(
                        rc_b[:, s_:s_ + 1], bank[:, off + 128:off + 129]),
                        inc=("dve", 1), waits=[lastPV[16 + s_ - 16] if False else lastPV[s_]])
                    nrm_ev = P.op("scalar", lambda s, bank=bank, off=off, s_=s_, i=i: s.activation(
                        at_b[s_ % 2][:], bank[:, off:off + 128], AF.Copy,
                        scale=rc_b[:, s_:s_ + 1]),
                        inc=("act", 1), waits=[rce, tr_ev.get(s_ - 2)])
                    trw = [nrm_ev]
                    if s_ in (0, 4):
                        trw.append(exp_evs[-1])
                    if s_ in (8, 12):
                        trw.append(grp_evac[s_ // 4 - 2])
                    tr_ev[s_] = P.op("tensor", lambda te, s_=s_, i=i: te.matmul(
                        sc[(s_ // 4) % 2][:, 128 * (s_ % 4):128 * (s_ % 4) + 128],
                        at_b[s_ % 2][:], id_bf[:], start=True, stop=True),
                        inc=("pe", 1), waits=trw)
                    if s_ % 4 == 3:
                        gq_ = s_ // 4
                        grp_evac[gq_] = P.op("scalar", lambda s, gq_=gq_, ast=ast: s.activation(
                            ast[:, 512 * gq_:512 * (gq_ + 1)], sc[gq_ % 2][:], AF.Copy),
                            inc=("act", 1), waits=[tr_ev[4 * gq_ + 3], ast_dma.get(h - 2)])
                head_evac_done[h] = nrm_ev
                head_trev_done[h] = grp_evac[3]
                ast_dma[h] = P.dma("scalar", attnT_s[h], ast[:], waits=[grp_evac[3]])
            if "attnT" in dbg:
                aq_all = ("aq", P.counts["aq"])
                for h in range(H):
                    P.dma("scalar", dbg["attnT"][h], attnT_s[h], waits=[aq_all])
            with nc.Block() as blk:
                P.emit(blk)

        P.new_block()
        with contextlib.ExitStack() as ph:
            psb = lambda n, s, dt=F32: ph.enter_context(nc.sbuf_tensor(n, s, dt))
            pps = lambda n, s: ph.enter_context(nc.psum_tensor(n, s, F32))
            xo_sb = psb("p4xo", [128, 16 * 2048], BF16)
            wg_sl = [psb(f"p4wg{i}", [128, 16 * 512], BF16) for i in range(2)]
            wu_sl = [psb(f"p4wu{i}", [128, 16 * 512], BF16) for i in range(2)]
            silu_b = [psb(f"p4si{i}", [128, 1024], BF16) for i in range(2)]
            gat_b = [psb(f"p4ga{i}", [128, 2048], BF16) for i in range(2)]
            ps_g = [pps(f"p4pg{i}", [128, 1024]) for i in range(2)]
            ps_u = [pps(f"p4pu{i}", [128, 1024]) for i in range(2)]

            p3_aq = ("aq", P.counts["aq"])
            p3_sq = ("sq", P.counts["sq"])
            xo_ev = None
            for c in range(16):
                xo_ev = P.dma("sync", xo_sb[:, 2048 * c:2048 * (c + 1)],
                              xnT_s[c][:, 0:2048], waits=[p3_aq])
            sl_ld = {}
            sl_used = {}

            def load_slab(sl):
                ev = None
                for j, (wt, dst) in enumerate(((wgT, wg_sl[sl % 2]), (wuT, wu_sl[sl % 2]))):
                    src = wt[:, 512 * sl:512 * (sl + 1)].rearrange("(a p) w -> p a w", p=128)
                    d3 = dst[:].rearrange("p (c w) -> p c w", w=512)
                    for half in range(2):
                        ev = P.dma("gpsimd", d3[:, 8 * half:8 * (half + 1), :],
                                   src[:, 8 * half:8 * (half + 1), :],
                                   waits=[sl_used.get(sl - 2), consts_gq, p3_sq])
                sl_ld[sl] = ev

            load_slab(0)
            gdma = {}
            evac_ev = {}
            for hc in range(64):
                sl, hl = divmod(hc, 4)
                if hl == 0 and sl + 1 < 16:
                    load_slab(sl + 1)
                gb = gat_b[hc % 2]
                for hf in range(2):
                    pg, pu = ps_g[hf], ps_u[hf]
                    mm = None
                    for c in range(16):
                        for m2 in range(2):
                            w0 = [sl_ld[sl], xo_ev, evac_ev.get((hc - 1, hf))] if (c == 0 and m2 == 0) else []
                            mm = P.op("tensor", lambda te, pg=pg, c=c, m2=m2, sl=sl, hl=hl, hf=hf: te.matmul(
                                pg[:, 512 * m2:512 * (m2 + 1)],
                                wg_sl[sl % 2][:, 512 * c + 128 * hl:512 * c + 128 * (hl + 1)],
                                xo_sb[:].rearrange("p (c w) -> p c w", w=2048)[:, c, 1024 * hf + 512 * m2:1024 * hf + 512 * (m2 + 1)],
                                start=(c == 0), stop=(c == 15)),
                                inc=None, waits=w0)
                    for c in range(16):
                        for m2 in range(2):
                            mm = P.op("tensor", lambda te, pu=pu, c=c, m2=m2, sl=sl, hl=hl, hf=hf: te.matmul(
                                pu[:, 512 * m2:512 * (m2 + 1)],
                                wu_sl[sl % 2][:, 512 * c + 128 * hl:512 * c + 128 * (hl + 1)],
                                xo_sb[:].rearrange("p (c w) -> p c w", w=2048)[:, c, 1024 * hf + 512 * m2:1024 * hf + 512 * (m2 + 1)],
                                start=(c == 0), stop=(c == 15)),
                                inc=("pe", 1) if (c == 15 and m2 == 1) else None, waits=[])
                    sl_used[sl] = mm
                    sie = P.op("scalar", lambda s, pg=pg, hf=hf, hc=hc: s.activation(
                        silu_b[hf][:], pg[:], AF.Silu),
                        inc=("act", 1), waits=[mm])
                    evac_ev[(hc, hf)] = P.op("vector", lambda v, gb=gb, hf=hf, pu=pu: v.tensor_tensor(
                        gb[:, 1024 * hf:1024 * (hf + 1)], silu_b[hf][:], pu[:], ALU.mult),
                        inc=("dve", 1), waits=[sie, gdma.get(hc - 2)])
                gdma[hc] = P.dma("scalar", gatedT_s[hc], gb[:],
                                 waits=[evac_ev[(hc, 0)], evac_ev[(hc, 1)]])
            if "gatedT" in dbg:
                aq_all = ("aq", P.counts["aq"])
                for hc in range(64):
                    P.dma("scalar", dbg["gatedT"][hc], gatedT_s[hc], waits=[aq_all])
            with nc.Block() as blk:
                P.emit(blk)

        P.new_block()
        with contextlib.ExitStack() as ph:
            psb = lambda n, s, dt=F32: ph.enter_context(nc.sbuf_tensor(n, s, dt))
            pps = lambda n, s: ph.enter_context(nc.psum_tensor(n, s, F32))
            at_sb = psb("p5at", [128, 16 * 2048], BF16)
            gt_sb = [psb(f"p5gt{i}", [128, 32 * 256], BF16) for i in range(3)]
            wq_b = [psb(f"p5wq{i}", [128, 4 * 2048], BF16) for i in range(4)]
            scrap_sb = psb("p5scrap", [128, 16])
            y_b = [psb(f"p5y{i}", [128, 2048]) for i in range(2)]
            ps_y = [pps(f"p5py{i}", [128, 2048]) for i in range(2)]

            at_ev = None
            for hh in range(16):
                at_ev = P.dma("sync", at_sb[:, 2048 * hh:2048 * (hh + 1)], attnT_s[hh],
                              waits=[ast_dma.get(hh)])
            gt_ev = {}
            wd_used = {}
            y_dma = {}
            yev = {}

            def load_gt(sp, half):
                ev = None
                gts = gt_sb[(2 * sp + half) % 3]
                d3 = gts[:].rearrange("p (c w) -> p c w", w=256)
                lane = f"gl{(2 * sp + half) % 3}"
                for jj in range(32):
                    hc = 32 * half + jj
                    ev = P.dma("sync", d3[:, jj, :], gatedT_s[hc][:, 256 * sp:256 * (sp + 1)],
                               waits=[gdma[hc] if sp == 0 else None,
                                      gt_used.get((2 * sp + half) - 3)], sem=lane)
                gt_ev[(sp, half)] = ev

            WDQ = {0: "sync", 1: "gpsimd", 2: "scalar"}
            NQUAD = 20

            def load_quad(sp, qd):
                if qd < 16:
                    s2 = wd_bf_s[512 * qd:512 * (qd + 1), :]
                    cwait = conv_st.get(4 * qd + 3)
                else:
                    s2 = wo_bf_s[512 * (qd - 16):512 * (qd - 15), :]
                    cwait = conv_st.get(64 + 4 * (qd - 16) + 3)
                src3 = s2.rearrange("(a p) w -> p a w", p=128)
                dst3 = wq_b[qd % 4][:].rearrange("p (a w) -> p a w", w=2048)
                prev_same_buf = (qd % 3) + 3 * ((NQUAD - 1 - (qd % 3)) // 3)
                return P.dma(WDQ[qd % 3], dst3, src3,
                             waits=[cwait,
                                    wd_used.get((sp, qd - 3)) or wd_used.get((sp - 1, prev_same_buf))])

            gt_used = {}
            load_gt(0, 0)
            load_gt(0, 1)
            load_gt(1, 0)
            wd_ld = {}
            x_ld = {}
            for sp in range(8):
                mme = {0: None, 1: None}
                for ti in range(2):
                    t = 2 * sp + ti
                    x_ld[(sp, ti)] = P.dma("gpsimd", y_b[ti][:], x_perm[128 * t:128 * (t + 1), :],
                                           waits=[y_dma.get((sp - 1, ti))], sem=f"xl{ti}")
                if sp == 0:
                    for pre in range(2):
                        wd_ld[(sp, pre)] = load_quad(sp, pre)
                for hc in range(64):
                    qd = hc // 4
                    if hc % 4 == 0 and qd + 2 < NQUAD:
                        wd_ld[(sp, qd + 2)] = load_quad(sp, qd + 2)
                    wde = wd_ld[(sp, qd)]
                    half = hc // 32
                    gts = gt_sb[(2 * sp + half) % 3]
                    for ti in range(2):
                        for m4 in range(4):
                            mme[ti] = P.op("tensor", lambda te, ti=ti, hc=hc, m4=m4, gts=gts, sp=sp: te.matmul(
                                ps_y[ti][:, 512 * m4:512 * (m4 + 1)],
                                gts[:, 256 * (hc % 32) + 128 * ti:256 * (hc % 32) + 128 * (ti + 1)],
                                wq_b[(hc // 4) % 4][:, 2048 * (hc % 4) + 512 * m4:2048 * (hc % 4) + 512 * (m4 + 1)],
                                start=(hc == 0), stop=False),
                                inc=("pe", 1) if (ti == 1 and m4 == 3) else None,
                                waits=([wde, gt_ev[(sp, half)], yev.get((sp - 1, ti))] if m4 == 0 else []))
                    if hc % 4 == 3:
                        wd_used[(sp, hc // 4)] = mme[1]
                    if hc % 32 == 31:
                        gt_used[2 * sp + half] = mme[1]
                    if hc == 32 and sp + 1 < 8:
                        load_gt(sp + 1, 1)
                for dc in range(16):
                    ch = 64 + dc
                    qd = ch // 4
                    if dc == 0 and sp + 2 < 8:
                        load_gt(sp + 2, 0)
                    if dc == 4 and sp + 1 < 8:
                        wd_ld[(sp + 1, 0)] = load_quad(sp + 1, 0)
                    if dc == 8 and sp + 1 < 8:
                        wd_ld[(sp + 1, 1)] = load_quad(sp + 1, 1)
                    if dc % 4 == 0 and qd + 2 < NQUAD:
                        wd_ld[(sp, qd + 2)] = load_quad(sp, qd + 2)
                    woe = wd_ld[(sp, qd)]
                    woe2 = None
                    for ti in range(2):
                        t = 2 * sp + ti
                        for m4 in range(4):
                            mme[ti] = P.op("tensor", lambda te, ti=ti, dc=dc, m4=m4, t=t: te.matmul(
                                ps_y[ti][:, 512 * m4:512 * (m4 + 1)],
                                at_sb[:, 2048 * dc + 128 * t:2048 * dc + 128 * (t + 1)],
                                wq_b[((64 + dc) // 4) % 4][:, 2048 * (dc % 4) + 512 * m4:2048 * (dc % 4) + 512 * (m4 + 1)],
                                start=False, stop=False),
                                inc=("pe", 1) if (ti == 1 and m4 == 3) else None,
                                waits=([woe, woe2, at_ev] if m4 == 0 else []))
                    if dc % 4 == 3:
                        wd_used[(sp, (64 + dc) // 4)] = mme[1]
                for ti in range(2):
                    t = 2 * sp + ti
                    for m4 in range(4):
                        mme[ti] = P.op("tensor", lambda te, ti=ti, m4=m4: te.matmul(
                            ps_y[ti][:, 512 * m4:512 * (m4 + 1)],
                            onesr_bf[0:1, :], bo_bf[0:1, 512 * m4:512 * (m4 + 1)],
                            start=False, stop=True),
                            inc=("pe", 1) if m4 == 3 else None, waits=[])
                    pass
                for ti in range(2):
                    t = 2 * sp + ti
                    P.op("vector", lambda v, ti=ti: v.tensor_copy(scrap_sb[:, 0:16], ps_y[ti][:, 0:16]),
                         waits=[mme[1], x_ld[(sp, ti)]])
                    ye = P.op("vector", lambda v, ti=ti: v.tensor_tensor(
                        y_b[ti][:], ps_y[ti][:], y_b[ti][:], ALU.add),
                        inc=("dve", 1))
                    yev[(sp, ti)] = ye
                    y_dma[(sp, ti)] = P.dma("scalar", y_out[128 * t:128 * (t + 1), :], y_b[ti][:],
                                            waits=[ye])
            if "at_sb" in dbg:
                last_y = y_dma[(7, 1)]
                P.dma("scalar", dbg["at_sb"], at_sb[:], waits=[last_y])
                P.dma("scalar", dbg["gt_sb"], gt_sb[(2 * 7 + 1) % 3][:], waits=[last_y])
            for s_ in ("aq", "sq", "gq", "hq0", "hq1", "wl0", "wl1", "wl2", "wl3",
                       "gl0", "gl1", "gl2", "xl0", "xl1"):
                P.op("gpsimd", lambda e, s_=s_: e.wait_ge(sems[s_], P.counts[s_]))
            with nc.Block() as blk:
                P.emit(blk)

    return nc


_NC_CACHE = {}


def get_nc():
    key = tuple(DBG)
    if key not in _NC_CACHE:
        _NC_CACHE[key] = build_nc()
    return _NC_CACHE[key]


def make_inputs(L0_attn_x, L0_attn_w_qkv, L0_attn_b_qkv, L0_attn_w_out, L0_attn_b_out,
                L0_ffn_w_gate, L0_ffn_w_up, L0_ffn_w_down):
    x = np.asarray(L0_attn_x, dtype=np.float32)
    w_qkv = np.asarray(L0_attn_w_qkv, dtype=np.float32)
    b_qkv = np.asarray(L0_attn_b_qkv, dtype=np.float32)
    w_out = np.asarray(L0_attn_w_out, dtype=np.float32)
    b_out = np.asarray(L0_attn_b_out, dtype=np.float32)
    w_gate = np.asarray(L0_ffn_w_gate, dtype=np.float32)
    w_up = np.asarray(L0_ffn_w_up, dtype=np.float32)
    w_down = np.asarray(L0_ffn_w_down, dtype=np.float32)

    shared = {
        "wqT": np.ascontiguousarray(w_qkv[0:D].T),
        "wkT": np.ascontiguousarray(w_qkv[D:2 * D].T),
        "wvT": np.ascontiguousarray(w_qkv[2 * D:3 * D].T),
        "woT": np.ascontiguousarray(w_out.T),
        "wgT": np.ascontiguousarray(w_gate.T),
        "wuT": np.ascontiguousarray(w_up.T),
        "wdT": np.ascontiguousarray(w_down.T),
        "bq_t": np.ascontiguousarray(b_qkv[0:D].reshape(16, 128).T),
        "bk_t": np.ascontiguousarray(b_qkv[D:2 * D].reshape(16, 128).T),
        "bv_row": np.ascontiguousarray(b_qkv[2 * D:3 * D].reshape(1, D)),
        "bo_row": np.ascontiguousarray(b_out.reshape(1, D)),
        "tri_m": np.where(np.arange(128)[:, None] > np.arange(128)[None, :],
                          np.float32(-1e30), np.float32(0.0)),
        "ident": np.eye(128, dtype=np.float32),
        "ones_col": np.ones((128, 1), dtype=np.float32),
        "ones_row": np.ones((1, 128), dtype=np.float32),
        "neg_ones_row": -np.ones((1, 128), dtype=np.float32),
    }
    in_maps = []
    perms = []
    for c in range(8):
        b, h = divmod(c, 2)
        own = [2 * i + h for i in range(16)]
        oth = [2 * i + (1 - h) for i in range(16)]
        perm = own + oth
        perms.append(own)
        xb = x[b].reshape(NT, 128, D)[perm].reshape(S, D)
        mb = np.full((128, 128), -1e30, np.float32) if h == 0 else np.zeros((128, 128), np.float32)
        m = dict(shared)
        m["x_perm"] = np.ascontiguousarray(xb)
        m["mask_b"] = mb
        in_maps.append(m)
    return in_maps, perms


def kernel(L0_attn_x, L0_attn_w_qkv, L0_attn_b_qkv, L0_attn_w_out, L0_attn_b_out,
           L0_ffn_w_gate, L0_ffn_w_up, L0_ffn_w_down):
    in_maps, perms = make_inputs(L0_attn_x, L0_attn_w_qkv, L0_attn_b_qkv,
                                 L0_attn_w_out, L0_attn_b_out,
                                 L0_ffn_w_gate, L0_ffn_w_up, L0_ffn_w_down)
    nc = get_nc()
    res = run_bass_kernel_spmd(nc, in_maps, core_ids=list(range(8)))
    out = np.empty((4, S, D), dtype=np.float32)
    for c in range(8):
        b = c // 2
        yc = res.results[c]["y"].reshape(16, 128, D)
        for i, gt in enumerate(perms[c]):
            out[b, 128 * gt:128 * (gt + 1), :] = yc[i]
    return out
